# version 09
# baseline (speedup 1.0000x reference)
import os
import numpy as np
import concourse.bass as bass
import concourse.mybir as mybir
from concourse.bass_utils import run_bass_kernel_spmd

F32 = mybir.dt.float32
BF16 = mybir.dt.bfloat16
AF = mybir.ActivationFunctionType
ALU = mybir.AluOpType
AXX = mybir.AxisListType.X

P = 128
CH = 16
TC = 2048
H = 16
TAU = 512
NT = TC // TAU
TD = 256
NTD = TC // TD
EPS = 1e-5

DBG = os.environ.get("KDBG", "")


class Buf:
    __slots__ = ("w", "r")
    def __init__(self):
        self.w = None
        self.r = []


class Rec:
    SEMS = ("pe", "dv", "ac", "po", "qsp", "qgp", "qac", "cc")
    ENGS = ("pe", "dv", "ac", "po", "sp")
    QUEUE = {"sp": ("sp", "qsp"), "gp": ("po", "qgp"), "ac": ("ac", "qac")}

    QENG = {"qsp": "sp", "qgp": "po", "qac": "ac"}

    def __init__(self):
        self.cnt = {s: 0 for s in self.SEMS}
        self.ops = {k: [] for k in self.ENGS}
        self._bufs = {}
        self.fence_cnt = {"qsp": 0, "qgp": 0, "qac": 0}

    def buf(self, *key):
        b = self._bufs.get(key)
        if b is None:
            b = self._bufs[key] = Buf()
        return b

    def _waits(self, reads, writes, extra):
        m = {}
        def add(ev):
            if ev is not None and ev[1] > m.get(ev[0], 0):
                m[ev[0]] = ev[1]
        for ev in extra:
            add(ev)
        for b in reads:
            add(b.w)
        for b in writes:
            add(b.w)
            for ev in b.r:
                add(ev)
        for s in ("qsp", "qgp", "qac"):
            if s in m:
                if m[s] > self.fence_cnt[s]:
                    self.ops[self.QENG[s]].append(([(s, self.cnt[s])], None, None, 0))
                    self.fence_cnt[s] = self.cnt[s]
                m[s] = self.fence_cnt[s]
        return sorted(m.items())

    def op(self, eng, fn, reads=(), writes=(), waits=(), sem=None, amt=1):
        sem = sem or eng
        w = self._waits(reads, writes, waits)
        self.cnt[sem] += amt
        ev = (sem, self.cnt[sem])
        self.ops[eng].append((w, fn, sem, amt))
        for b in writes:
            b.w = ev
            b.r = []
        for b in reads:
            b.r.append(ev)
        return ev

    def dma(self, q, out, in_, reads=(), writes=(), waits=()):
        eng, sem = self.QUEUE[q]
        def fn(e, out=out, in_=in_):
            return e.dma_start(out=out, in_=in_)
        return self.op(eng, fn, reads, writes, waits, sem=sem, amt=16)

    def wait_only(self, eng, waits):
        self.ops[eng].append((self._waits((), (), waits), None, None, 0))

    def mark(self, sem):
        return (sem, self.cnt[sem])

    def barrier(self):
        marks = [self.mark(s) for s in self.SEMS if self.cnt[s] > 0]
        for eng in self.ENGS:
            self.wait_only(eng, marks)


def emit(nc, rec, sems):
    with nc.Block() as block:
        def mk(key):
            def body(e):
                last = {}
                for (waits, fn, sem, amt) in rec.ops[key]:
                    for (s, v) in waits:
                        if last.get(s, 0) < v:
                            e.wait_ge(sems[s], v)
                            last[s] = v
                    if fn is None:
                        continue
                    ins = fn(e)
                    if amt:
                        ins.then_inc(sems[sem], amt)
            return body
        block.tensor(mk("pe"))
        block.vector(mk("dv"))
        block.scalar(mk("ac"))
        block.gpsimd(mk("po"))
        block.sync(mk("sp"))


def build_program():
    nc = bass.Bass("TRN2", target_bir_lowering=False, debug=False)

    x_d = nc.dram_tensor("x", [TC, TC], F32, kind="ExternalInput").ap()
    wq_d = nc.dram_tensor("wq", [CH, P, 2048], F32, kind="ExternalInput").ap()
    wk_d = nc.dram_tensor("wk", [CH, P, 2048], F32, kind="ExternalInput").ap()
    wv_d = nc.dram_tensor("wv", [CH, P, 2048], F32, kind="ExternalInput").ap()
    wo_d = nc.dram_tensor("wo", [CH, P, 2048], F32, kind="ExternalInput").ap()
    fw_d = nc.dram_tensor("fw", [CH, P, 2048], F32, kind="ExternalInput").ap()
    la_d = nc.dram_tensor("la", [CH, P, 128], F32, kind="ExternalInput").ap()
    lb_d = nc.dram_tensor("lb", [P, 2048], F32, kind="ExternalInput").ap()
    rw_d = nc.dram_tensor("rw", [P, 128], F32, kind="ExternalInput").ap()
    d16_d = nc.dram_tensor("d16", [16, 2048], F32, kind="ExternalInput").ap()
    d8_d = nc.dram_tensor("d8", [8, 128], F32, kind="ExternalInput").ap()
    eye_d = nc.dram_tensor("eye", [P, P], F32, kind="ExternalInput").ap()
    out_d = nc.dram_tensor("out", [CH, P, 2048], F32, kind="ExternalOutput").ap()
    xmid_d = nc.dram_tensor("xmid", [CH, P, 2048], F32).ap()
    kvin_d = nc.dram_tensor("kvin", [P + 1, 2048], F32).ap()
    kvout_d = nc.dram_tensor("kvout", [P + 1, 2048], F32).ap()

    import contextlib
    es = contextlib.ExitStack()
    with es:
        r1 = nc.alloc_sbuf_tensor("r1", [P, 32768], BF16)
        r2 = nc.alloc_sbuf_tensor("r2", [P, 32768], BF16)
        r3 = nc.alloc_sbuf_tensor("r3", [P, 16384], BF16)
        r4 = nc.alloc_sbuf_tensor("r4", [P, 4096], BF16)
        r1b, r1f = r1.ap(), r1.ap().bitcast(F32)
        r2b = r2.ap()
        r3b = r3.ap()
        r4b, r4f = r4.ap(), r4.ap().bitcast(F32)

        phiq = r1b
        xn = r2b
        O = r1b
        fwT = r2b

        st = {}
        st["d16"] = nc.alloc_sbuf_tensor("d16b", [16, 2048], BF16).ap()
        st["d8"] = nc.alloc_sbuf_tensor("d8b", [8, 128], BF16).ap()
        st["eyeb"] = nc.alloc_sbuf_tensor("eyeb", [P, P], BF16).ap()
        st["eyef"] = nc.alloc_sbuf_tensor("eyef", [P, P], F32).ap()
        st["ones_c"] = nc.alloc_sbuf_tensor("ones_c", [P, 1], BF16).ap()
        st["ones_r"] = nc.alloc_sbuf_tensor("ones_r", [1, P], BF16).ap()
        st["rw"] = nc.alloc_sbuf_tensor("rwb", [P, 128], BF16).ap()
        st["la"] = nc.alloc_sbuf_tensor("lab", [P, 2048], BF16).ap()
        st["lb"] = nc.alloc_sbuf_tensor("lbb", [P, 2048], BF16).ap()
        st["wd"] = nc.alloc_sbuf_tensor("wdb", [P, 256], BF16).ap()
        st["kvb"] = nc.alloc_sbuf_tensor("kvb", [P, 2048], BF16).ap()
        st["kvp"] = nc.alloc_sbuf_tensor("kvp", [P, 2048], F32).ap()
        st["A"] = nc.alloc_sbuf_tensor("A_sb", [P, 512], BF16).ap()
        st["B"] = nc.alloc_sbuf_tensor("B_sb", [P, 512], BF16).ap()
        st["wby"] = nc.alloc_sbuf_tensor("wby", [P, 512], BF16).ap()
        st["zbs"] = nc.alloc_sbuf_tensor("zbs", [P, 1024], BF16).ap()
        st["ksr"] = nc.alloc_sbuf_tensor("ksr", [1, 2048], BF16).ap()
        st["zball"] = nc.alloc_sbuf_tensor("zball", [16, 2048], BF16).ap()
        st["eps"] = nc.alloc_sbuf_tensor("eps_c", [P, 1], F32).ap()
        nc.const_aps.register(F32, EPS, st["eps"][:])

        pb = [nc.alloc_psum_tensor(f"pb{i}", [P, 512], F32).ap() for i in range(8)]

        sems = {}
        for s in Rec.SEMS:
            sems[s] = es.enter_context(nc.semaphore(s))
        R = Rec()
        pbb = [R.buf("pb", i) for i in range(8)]

        def finish():
            R.wait_only("sp", [R.mark("qsp"), R.mark("qgp")])
            emit(nc, R, sems)
            return nc

        def dbg_dump_bf(src):
            R.barrier()
            for k in range(CH):
                R.dma("gp", out_d[k], src[:, k*2048:(k+1)*2048])
            return finish()

        def dbg_dump_kvp():
            R.barrier()
            R.dma("sp", out_d[0], st["kvp"][:, 0:2048])
            R.dma("sp", out_d[1, :, 0:16], st["kvp"][:, 2048:2064])
            return finish()

        bc = {}
        for nm, dst, src in [("d16", st["d16"][:], d16_d[:]),
                             ("d8", st["d8"][:], d8_d[:]),
                             ("eyeb", st["eyeb"][:], eye_d[:]),
                             ("rw", st["rw"][:], rw_d[:]),
                             ("lb", st["lb"][:], lb_d[:])]:
            bc[nm] = R.buf("c", nm)
            R.dma("gp", dst, src, writes=[bc[nm]])
        bc["eyef"] = R.buf("c", "eyef")
        R.dma("sp", st["eyef"][:], eye_d[:], writes=[bc["eyef"]])
        bc["la"] = R.buf("c", "la")
        for k in range(CH):
            R.dma("gp", st["la"][:, k*128:(k+1)*128], la_d[k], writes=[bc["la"]])
        bc["ones_c"] = R.buf("c", "ones_c")
        bc["ones_r"] = R.buf("c", "ones_r")
        R.op("po", lambda e: e.memset(st["ones_c"][:], 1.0), writes=[bc["ones_c"]])
        R.op("po", lambda e: e.memset(st["ones_r"][:], 1.0), writes=[bc["ones_r"]])
        bc["eps"] = R.buf("c", "eps")
        R.op("po", lambda e: e.memset(st["eps"][:], EPS), writes=[bc["eps"]])

        xnb = lambda k, it: R.buf("xn", k, it)
        for k in range(CH):
            R.dma("gp", xn[:, k*2048:(k+1)*2048], x_d[k*128:(k+1)*128, :],
                  writes=[xnb(k, 0), xnb(k, 1), xnb(k, 2), xnb(k, 3)])

        sq_a = r1b[:, 16512:24704]
        for it in range(NT):
            t0 = it * TAU
            xsl = lambda k, t0=t0: xn[:, k*2048 + t0: k*2048 + t0 + TAU]
            sqsl = lambda k: sq_a[:, k*512: k*512 + TAU]
            sqb = R.buf("sq_a")
            for k in range(CH):
                R.op("ac", (lambda e, i=xsl(k), o=sqsl(k): e.activation(o, i, AF.Square)),
                     reads=[xnb(k, it)], writes=[sqb])
            ps_s, ps_q = pb[0][0:1, 0:TAU], pb[1][0:1, 0:TAU]
            def g_s(e, ps=ps_s, xsl=xsl):
                for k in range(CH):
                    ins = e.matmul(ps, st["ones_c"][:], xsl(k),
                                   start=(k == 0), stop=(k == CH - 1))
                return ins
            def g_q(e, ps=ps_q, sqsl=sqsl):
                for k in range(CH):
                    ins = e.matmul(ps, st["ones_c"][:], sqsl(k),
                                   start=(k == 0), stop=(k == CH - 1))
                return ins
            R.op("pe", g_s, reads=[xnb(k, it) for k in range(CH)] + [bc["ones_c"]],
                 writes=[pbb[0]])
            R.op("pe", g_q, reads=[sqb, bc["ones_c"]], writes=[pbb[1]])
            rm = r4f[0:1, 0:TAU]
            rq = r4f[0:1, 512:512 + TAU]
            rt = r4f[0:1, 1024:1024 + TAU]
            abf = r4b[0:1, 3072:3072 + TAU]
            bbf = r4b[0:1, 3584:3584 + TAU]
            bm, bq, bt = R.buf("rm"), R.buf("rq"), R.buf("rt")
            babf, bbbf = R.buf("abf"), R.buf("bbf")
            R.op("dv", lambda e, o=rm, i=ps_s: e.tensor_scalar(o, i, 1.0/2048, None, ALU.mult),
                 reads=[pbb[0]], writes=[bm])
            R.op("dv", lambda e, o=rq, i=ps_q: e.tensor_scalar(o, i, 1.0/2048, None, ALU.mult),
                 reads=[pbb[1]], writes=[bq])
            R.op("dv", lambda e, o=rt, a=rm: e.tensor_tensor(o, a, a, ALU.mult),
                 reads=[bm], writes=[bt])
            R.op("dv", lambda e, o=rq, a=rq, b=rt: e.tensor_tensor(o, a, b, ALU.subtract),
                 reads=[bq, bt], writes=[bq])
            R.op("ac", lambda e, o=rt, i=rq: e.activation(o, i, AF.Ln, bias=EPS),
                 reads=[bq, bc["eps"]], writes=[bt])
            R.op("ac", lambda e, o=rt, i=rt: e.activation(o, i, AF.Exp, scale=-0.5),
                 reads=[bt], writes=[bt])
            R.op("dv", lambda e, o=rm, a=rm, b=rt:
                 e.scalar_tensor_tensor(o, a, -1.0, b, ALU.mult, ALU.mult),
                 reads=[bm, bt], writes=[bm])
            R.op("dv", lambda e, o=abf, i=rt: e.tensor_copy(o, i), reads=[bt], writes=[babf])
            R.op("dv", lambda e, o=bbf, i=rm: e.tensor_copy(o, i), reads=[bm], writes=[bbbf])
            ps_A, ps_B = pb[2][:, 0:TAU], pb[3][:, 0:TAU]
            R.op("pe", lambda e, o=ps_A, r=abf: e.matmul(o, st["ones_r"][:], r, start=True, stop=True),
                 reads=[babf, bc["ones_r"]], writes=[pbb[2]])
            R.op("pe", lambda e, o=ps_B, r=bbf: e.matmul(o, st["ones_r"][:], r, start=True, stop=True),
                 reads=[bbbf, bc["ones_r"]], writes=[pbb[3]])
            Asl, Bsl = st["A"][:, 0:TAU], st["B"][:, 0:TAU]
            bA, bB = R.buf("A_sb"), R.buf("B_sb")
            R.op("dv", lambda e, o=Asl, i=ps_A: e.tensor_copy(o, i), reads=[pbb[2]], writes=[bA])
            R.op("dv", lambda e, o=Bsl, i=ps_B: e.tensor_copy(o, i), reads=[pbb[3]], writes=[bB])
            for k in range(CH):
                R.op("dv", lambda e, a=xsl(k), b=Bsl: e.tensor_tensor(a, a, b, ALU.add),
                     reads=[bB], writes=[xnb(k, it)])
                R.op("dv", lambda e, a=xsl(k), b=Asl: e.tensor_tensor(a, a, b, ALU.mult),
                     reads=[bA], writes=[xnb(k, it)])

        if DBG == "A":
            return dbg_dump_bf(xn)

        wk_bufs = (r3b[:, 0:8192], r1b[:, 24576:32768])
        wv_bufs = (r3b[:, 8192:16384], r1b[:, 16384:24576])
        ev_a_ac, ev_a_pe = R.mark("ac"), R.mark("pe")
        fko = r1b[:, 0:8192]
        vo = r1b[:, 8192:16384]
        bfko, bvo = R.buf("fko"), R.buf("vo")
        KOT = int(os.environ.get("KOT", "4"))
        for ot in range(KOT):
            wk_sb = wk_bufs[ot % 2]
            wv_sb = wv_bufs[ot % 2]
            for tc in range(CH):
                db = tc % 2
                psk = pb[4 + db][:, 0:512]
                psv = pb[6 + db][:, 0:512]
                bk, bv = pbb[4 + db], pbb[6 + db]
                if tc == 8 and ot + 1 < KOT:
                    nk = wk_bufs[(ot + 1) % 2]
                    nv = wv_bufs[(ot + 1) % 2]
                    for k in range(CH):
                        R.dma("gp", nv[:, k*512:(k+1)*512],
                              wv_d[k, :, (ot+1)*512:(ot+2)*512],
                              writes=[R.buf("wv", (ot + 1) % 2, k)],
                              waits=[ev_a_ac, ev_a_pe] if (ot == 0 and k == 0) else ())
                        R.dma("gp", nk[:, k*512:(k+1)*512],
                              wk_d[k, :, (ot+1)*512:(ot+2)*512],
                              writes=[R.buf("wk", (ot + 1) % 2, k)])
                for j in range(4):
                    if tc == 0 and ot == 0:
                        for k in range(j*4, (j+1)*4):
                            R.dma("gp", wk_sb[:, k*512:(k+1)*512],
                                  wk_d[k, :, ot*512:(ot+1)*512], writes=[R.buf("wk", 0, k)])
                            R.dma("gp", wv_sb[:, k*512:(k+1)*512],
                                  wv_d[k, :, ot*512:(ot+1)*512], writes=[R.buf("wv", 0, k)])
                    def g_kv(e, pk=psk, pv=psv, wk_sb=wk_sb, wv_sb=wv_sb, j=j, tc=tc):
                        for k in range(j*4, j*4 + 4):
                            lhs = xn[:, k*2048 + tc*128: k*2048 + (tc+1)*128]
                            e.matmul(pk, lhs, wk_sb[:, k*512:(k+1)*512],
                                     start=(k == 0), stop=(k == CH - 1))
                            ins = e.matmul(pv, lhs, wv_sb[:, k*512:(k+1)*512],
                                           start=(k == 0), stop=(k == CH - 1))
                        return ins
                    R.op("pe", g_kv,
                         reads=[R.buf("wk", ot % 2, k) for k in range(j*4, j*4 + 4)]
                               + [R.buf("wv", ot % 2, k) for k in range(j*4, j*4 + 4)]
                               + [xnb(k, tc // 4) for k in range(j*4, j*4 + 4)],
                         writes=[bk, bv])
                t1 = st["zbs"][:, db*512:(db + 1)*512]
                bt1 = R.buf("bt1", db)
                R.op("dv", lambda e, o=t1, i=psk: e.tensor_scalar(o, i, 0.0, None, ALU.min),
                     reads=[bk], writes=[bt1])
                R.op("ac", lambda e, o=t1: e.activation(o, o, AF.Exp),
                     reads=[bt1], writes=[bt1])
                R.op("dv", (lambda e, o=fko[:, tc*512:(tc+1)*512], i=psk, x=t1:
                            e.scalar_tensor_tensor(o, i, 0.0, x, ALU.max, ALU.add)),
                     reads=[bk, bt1], writes=[bfko])
                R.op("dv", (lambda e, o=vo[:, tc*512:(tc+1)*512], i=psv: e.tensor_copy(o, i)),
                     reads=[bv], writes=[bvo])
            for i4 in range(4):
                pskv = pb[0 + i4][:, 0:128]
                def g_kvh(e, ps=pskv, i4=i4):
                    for tc in range(CH):
                        ins = e.matmul(ps, fko[:, tc*512 + i4*128: tc*512 + (i4+1)*128],
                                       vo[:, tc*512 + i4*128: tc*512 + (i4+1)*128],
                                       start=(tc == 0), stop=(tc == CH - 1))
                    return ins
                R.op("pe", g_kvh, reads=[bfko, bvo], writes=[pbb[0 + i4]])
            ps_ks = pb[4][0:1, 0:512]
            def g_ks(e, ps=ps_ks):
                for tc in range(CH):
                    ins = e.matmul(ps, st["ones_c"][:], fko[:, tc*512:(tc+1)*512],
                                   start=(tc == 0), stop=(tc == CH - 1))
                return ins
            R.op("pe", g_ks, reads=[bfko, bc["ones_c"]], writes=[pbb[4]])
            for i4 in range(4):
                hh = ot*4 + i4
                R.op("dv", (lambda e, o=st["kvp"][:, hh*128:(hh+1)*128], i=pb[0+i4][:, 0:128]:
                            e.tensor_copy(o, i)),
                     reads=[pbb[0 + i4]], writes=[R.buf("kvp")])
            R.op("dv", (lambda e, o=st["ksr"][0:1, ot*512:(ot+1)*512], i=ps_ks:
                        e.tensor_copy(o, i)),
                 reads=[pbb[4]], writes=[R.buf("ksr")])

        if DBG == "B2F":
            R.barrier()
            for k in range(4):
                R.dma("gp", out_d[k], fko[:, k*2048:(k+1)*2048])
            for k in range(4):
                R.dma("gp", out_d[4 + k], vo[:, k*2048:(k+1)*2048])
            return finish()
        if DBG == "B2":
            return dbg_dump_kvp()
        R.barrier()

        R.dma("sp", kvin_d[0:P], st["kvp"][:], reads=[R.buf("kvp")], writes=[R.buf("kvin")])
        R.dma("gp", kvin_d[P:P+1], st["ksr"][:], reads=[R.buf("ksr")], writes=[R.buf("kvin")])
        R.op("po", lambda e: e.collective_compute(
                "AllReduce", ALU.add,
                replica_groups=[[0, 1], [2, 3], [4, 5], [6, 7]],
                ins=[kvin_d[:]], outs=[kvout_d[:]]),
             reads=[R.buf("kvin")], writes=[R.buf("kvout")], sem="cc", amt=1)

        for h in range(H):
            db = h % 2
            base = db * 4
            wq_ap = r3b[:, db*2048:(db + 1)*2048]
            wb = R.buf("wq", db)
            for j in range(4):
                R.dma("gp", wq_ap[:, j*512:(j+1)*512],
                      wq_d[j*4:(j+1)*4, :, h*128:(h+1)*128].rearrange("k p o -> p k o"),
                      writes=[wb])
            def g_qp(e, w=wq_ap, base=base):
                for k in range(CH):
                    for itau in range(NT):
                        ins = e.matmul(pb[base + itau][:, 0:TAU], w[:, k*128:(k+1)*128],
                                       xn[:, k*2048 + itau*TAU: k*2048 + itau*TAU + TAU],
                                       start=(k == 0), stop=(k == CH - 1))
                return ins
            R.op("pe", g_qp,
                 reads=[wb] + [xnb(k, i2) for k in range(CH) for i2 in range(NT)],
                 writes=[pbb[base + i] for i in range(NT)])
            for itau in range(NT):
                psq = pb[base + itau][:, 0:TAU]
                pbuf = pbb[base + itau]
                t1 = r4b[:, (itau % 2)*512:(itau % 2)*512 + TAU]
                e1 = r4b[:, 1024 + (itau % 2)*512:1024 + (itau % 2)*512 + TAU]
                bt1, be1 = R.buf("t1", itau % 2), R.buf("e1", itau % 2)
                R.op("dv", lambda e, o=t1, i=psq: e.tensor_scalar(o, i, 0.0, None, ALU.min),
                     reads=[pbuf], writes=[bt1])
                R.op("ac", lambda e, o=e1, i=t1: e.activation(o, i, AF.Exp),
                     reads=[bt1], writes=[be1])
                phis = phiq[:, h*2048 + itau*TAU: h*2048 + itau*TAU + TAU]
                R.op("dv", lambda e, o=phis, i=psq, x=e1:
                     e.scalar_tensor_tensor(o, i, 0.0, x, ALU.max, ALU.add),
                     reads=[pbuf, be1], writes=[R.buf("phiq", h, itau)])

        if DBG == "B1":
            return dbg_dump_bf(phiq)
        ev_b1_pe = R.mark("pe")

        R.dma("sp", st["kvp"][:], kvout_d[0:P], reads=[R.buf("kvout")], writes=[R.buf("kvp")])
        R.dma("gp", st["ksr"][:], kvout_d[P:P+1], reads=[R.buf("kvout")], writes=[R.buf("ksr")])
        R.op("dv", (lambda e, o=st["kvb"][:], i=st["kvp"][:]: e.tensor_copy(o, i)),
             reads=[R.buf("kvp")], writes=[R.buf("kvb")])
        R.op("po", lambda e: e.memset(st["wd"][:], 0.0), writes=[R.buf("wd")])
        for h in range(H):
            pst = pb[5].bitcast(BF16)[:, 0:1]
            R.op("pe", (lambda e, o=pst, i=st["ksr"][0:1, h*128:(h+1)*128]:
                        e.transpose(o, i, st["eyeb"][0:1, 0:1])),
                 reads=[R.buf("ksr"), bc["eyeb"]], writes=[pbb[5]])
            R.op("dv", (lambda e, o=st["wd"][:, h*16+h: h*16+h+1], i=pst:
                        e.tensor_copy(o, i)),
                 reads=[pbb[5]], writes=[R.buf("wd")])

        if DBG == "AG":
            return dbg_dump_kvp()
        R.barrier()

        for m in range(4):
            wo_ap = r3b[:, (m % 4)*2048:((m % 4) + 1)*2048]
            wb = R.buf("wo", m % 4)
            for j in range(4):
                R.dma("gp", wo_ap[:, j*512:(j+1)*512],
                      wo_d[j*4:(j+1)*4, :, m*128:(m+1)*128].rearrange("k p o -> p k o"),
                      writes=[wb], waits=[ev_b1_pe] if (m == 0 and j == 0) else ())
        for k in range(CH):
            R.dma("gp", fwT[:, k*2048:(k+1)*2048], fw_d[k], writes=[R.buf("fw", k)])

        zball = st["zball"]
        bzall = R.buf("zball")
        for it in range(NT):
            t0 = it * TAU
            ps_den = pb[0][0:16, 0:TAU]
            def g_den(e, ps=ps_den, t0=t0):
                for k in range(CH):
                    ins = e.matmul(ps, st["wd"][:, k*16:(k+1)*16],
                                   phiq[:, k*2048 + t0: k*2048 + t0 + TAU],
                                   start=(k == 0), stop=(k == CH - 1))
                return ins
            R.op("pe", g_den,
                 reads=[R.buf("wd")] + [R.buf("phiq", h, it) for h in range(H)],
                 writes=[pbb[0]])
            zr = r4f[0:16, 512:512 + TAU]
            bz = R.buf("zr")
            R.op("dv", lambda e, o=zr, i=ps_den: e.tensor_scalar(o, i, 1e-6, None, ALU.add),
                 reads=[pbb[0]], writes=[bz])
            R.op("dv", lambda e, o=zr: e.reciprocal(o, o), reads=[bz], writes=[bz])
            R.op("dv", (lambda e, o=zball[:, it*TAU:(it+1)*TAU], i=zr: e.tensor_copy(o, i)),
                 reads=[bz], writes=[bzall])
        for it in range(NT):
            t0 = it * TAU
            for h in range(H):
                db = h % 2
                ps_zb = pb[1 + db][:, 0:TAU]
                ps_at = pb[3 + db][:, 0:TAU]
                bzb, bat = pbb[1 + db], pbb[3 + db]
                R.op("pe", (lambda e, o=ps_zb, l=st["d16"][:, h*128:(h+1)*128],
                            r=zball[:, it*TAU:(it+1)*TAU]:
                            e.matmul(o, l, r, start=True, stop=True)),
                     reads=[bzall, bc["d16"]], writes=[bzb])
                R.op("pe", (lambda e, o=ps_at, l=st["kvb"][:, h*128:(h+1)*128],
                            r=phiq[:, h*2048 + t0: h*2048 + t0 + TAU]:
                            e.matmul(o, l, r, start=True, stop=True)),
                     reads=[R.buf("kvb"), R.buf("phiq", h, it)], writes=[bat])
                zbs = st["zbs"][:, db*512: db*512 + TAU]
                bzbs = R.buf("zbs", db)
                R.op("dv", lambda e, o=zbs, i=ps_zb: e.tensor_copy(o, i),
                     reads=[bzb], writes=[bzbs])
                osl = O[:, h*2048 + t0: h*2048 + t0 + TAU]
                R.op("dv", lambda e, o=osl, i=ps_at, z=zbs: e.tensor_tensor(o, i, z, ALU.mult),
                     reads=[bat, bzbs], writes=[R.buf("phiq", h, it)])

        if DBG == "C1":
            return dbg_dump_bf(O)

        for m in range(CH):
            db = m % 4
            wo_ap = r3b[:, db*2048:(db + 1)*2048]
            wb = R.buf("wo", db)
            if m >= 4:
                for j in range(4):
                    R.dma("gp", wo_ap[:, j*512:(j+1)*512],
                          wo_d[j*4:(j+1)*4, :, m*128:(m+1)*128].rearrange("k p o -> p k o"),
                          writes=[wb])

            for it in range(NT):
                t0 = it * TAU
                idx = m * NT + it
                def xt_load(ix):
                    if ix >= CH * NT:
                        return
                    mm, ii = ix // NT, ix % NT
                    xt_ = r3b.bitcast(F32)[:, 4096 + (ix % 4)*512: 4096 + (ix % 4)*512 + TAU]
                    R.dma("ac", xt_, x_d[mm*128:(mm+1)*128, ii*TAU:(ii+1)*TAU],
                          writes=[R.buf("xt", ix % 4)])
                if idx == 0:
                    for ix in range(3):
                        xt_load(ix)
                xt_load(idx + 3)
                ps = pb[5 + (it % 2)][:, 0:TAU]
                pbuf = pbb[5 + (it % 2)]
                def g_wo(e, ps=ps, w=wo_ap, t0=t0):
                    for k in range(CH):
                        ins = e.matmul(ps, w[:, k*128:(k+1)*128],
                                       O[:, k*2048 + t0: k*2048 + t0 + TAU],
                                       start=(k == 0), stop=(k == CH - 1))
                    return ins
                R.op("pe", g_wo, reads=[wb] + [R.buf("phiq", k, it) for k in range(CH)],
                     writes=[pbuf])
                xt = r3b.bitcast(F32)[:, 4096 + (idx % 4)*512: 4096 + (idx % 4)*512 + TAU]
                xm = r3b.bitcast(F32)[:, 6144 + (idx % 4)*512: 6144 + (idx % 4)*512 + TAU]
                bxt, bxm = R.buf("xt", idx % 4), R.buf("xm", idx % 4)
                R.op("dv", lambda e, o=xm, i=ps, x=xt: e.tensor_tensor(o, i, x, ALU.add),
                     reads=[pbuf, bxt], writes=[bxm])
                R.dma("sp", xmid_d[m, :, t0:t0 + TAU], xm, reads=[bxm],
                      writes=[R.buf("xmid_d", m)])

        if DBG == "C2":
            R.barrier()
            for m in range(CH):
                R.dma("sp", out_d[m], xmid_d[m], reads=[R.buf("xmid_d", m)])
            return finish()
        R.barrier()

        sqd = r3b[:, 12288:16384]
        xn2_b = [r3b[:, 0:4096], r3b[:, 4096:8192], r3b[:, 8192:12288]]
        xmf_b = [r1f[:, 0:4096], r1f[:, 4096:8192], r1f[:, 8192:12288]]
        xmb_b = [r1b[:, 24576:28672], r1b[:, 28672:32768]]
        wb_b = [r4b[:, 2048:2304], r4b[:, 2304:2560]]
        y_b = [r4b[:, 2560:2816], r4b[:, 2816:3072]]
        rows_f = st["ksr"].bitcast(F32)

        def d_load_xmf(s):
            if s >= NTD:
                return
            t0 = s * TD
            xmf, bxmf = xmf_b[s % 3], R.buf("xmf", s % 3)
            for j in range(4):
                R.dma("ac", xmf[:, j*4*TD:(j+1)*4*TD],
                      xmid_d[j*4:(j+1)*4, :, t0:t0 + TD].rearrange("k p t -> p k t"),
                      reads=[R.buf("xmid_d", m) for m in range(j*4, (j+1)*4)],
                      writes=[bxmf])

        def d_load_xmb(s):
            if s >= NTD:
                return
            t0 = s * TD
            xmb, bxmb = xmb_b[s % 2], R.buf("xmb", s % 2)
            for j in range(4):
                R.dma("gp", xmb[:, j*4*TD:(j+1)*4*TD],
                      xmid_d[j*4:(j+1)*4, :, t0:t0 + TD].rearrange("k p t -> p k t"),
                      reads=[R.buf("xmid_d", m) for m in range(j*4, (j+1)*4)],
                      writes=[bxmb])

        def d_chain1(s):
            xmf, xmb, xn2 = xmf_b[s % 3], xmb_b[s % 2], xn2_b[s % 3]
            bxmf, bxmb, bxn2 = R.buf("xmf", s % 3), R.buf("xmb", s % 2), R.buf("xn2", s % 3)
            bsqd = R.buf("sqd")
            for k in range(CH):
                R.op("ac", (lambda e, i=xmb[:, k*TD:(k+1)*TD], o=sqd[:, k*TD:(k+1)*TD]:
                            e.activation(o, i, AF.Square)),
                     reads=[bxmb], writes=[bsqd])
            ps_s, ps_q = pb[0][0:1, 0:TD], pb[1][0:1, 0:TD]
            def g_s2(e, ps=ps_s, xmb=xmb):
                for k in range(CH):
                    ins = e.matmul(ps, st["ones_c"][:], xmb[:, k*TD:(k+1)*TD],
                                   start=(k == 0), stop=(k == CH - 1))
                return ins
            def g_q2(e, ps=ps_q, sqd=sqd):
                for k in range(CH):
                    ins = e.matmul(ps, st["ones_c"][:], sqd[:, k*TD:(k+1)*TD],
                                   start=(k == 0), stop=(k == CH - 1))
                return ins
            R.op("pe", g_s2, reads=[bxmb, bc["ones_c"]], writes=[pbb[0]])
            R.op("pe", g_q2, reads=[bsqd, bc["ones_c"]], writes=[pbb[1]])
            return (ps_s, ps_q)

        def d_chain1b(s, ps_s, ps_q):
            xmf, xmb, xn2 = xmf_b[s % 3], xmb_b[s % 2], xn2_b[s % 3]
            bxmb, bxn2 = R.buf("xmb", s % 2), R.buf("xn2", s % 3)
            rm = rows_f[0:1, 0:TD]
            rq = rows_f[0:1, 256:256 + TD]
            rt = rows_f[0:1, 512:512 + TD]
            abf = r4b[0:1, 3224:3224 + TD]
            bbf = r4b[0:1, 3480:3480 + TD]
            bm, bq, bt = R.buf("rm2"), R.buf("rq2"), R.buf("rt2")
            babf, bbbf = R.buf("abf2"), R.buf("bbf2")
            R.op("dv", lambda e, o=rm, i=ps_s: e.tensor_scalar(o, i, 1.0/2048, None, ALU.mult),
                 reads=[pbb[0]], writes=[bm])
            R.op("dv", lambda e, o=rq, i=ps_q: e.tensor_scalar(o, i, 1.0/2048, None, ALU.mult),
                 reads=[pbb[1]], writes=[bq])
            R.op("dv", lambda e, o=rt, a=rm: e.tensor_tensor(o, a, a, ALU.mult),
                 reads=[bm], writes=[bt])
            R.op("dv", lambda e, o=rq, a=rq, b=rt: e.tensor_tensor(o, a, b, ALU.subtract),
                 reads=[bq, bt], writes=[bq])
            R.op("ac", lambda e, o=rt, i=rq: e.activation(o, i, AF.Ln, bias=EPS),
                 reads=[bq, bc["eps"]], writes=[bt])
            R.op("ac", lambda e, o=rt: e.activation(o, o, AF.Exp, scale=-0.5),
                 reads=[bt], writes=[bt])
            R.op("dv", lambda e, o=rm, a=rm, b=rt:
                 e.scalar_tensor_tensor(o, a, -1.0, b, ALU.mult, ALU.mult),
                 reads=[bm, bt], writes=[bm])
            R.op("dv", lambda e, o=abf, i=rt: e.tensor_copy(o, i), reads=[bt], writes=[babf])
            R.op("dv", lambda e, o=bbf, i=rm: e.tensor_copy(o, i), reads=[bm], writes=[bbbf])
            ps_A, ps_B = pb[0][:, 0:TD], pb[1][:, 0:TD]
            R.op("pe", lambda e, o=ps_A, r=abf: e.matmul(o, st["ones_r"][:], r, start=True, stop=True),
                 reads=[babf, bc["ones_r"]], writes=[pbb[0]])
            R.op("pe", lambda e, o=ps_B, r=bbf: e.matmul(o, st["ones_r"][:], r, start=True, stop=True),
                 reads=[bbbf, bc["ones_r"]], writes=[pbb[1]])
            Asl, Bsl = st["A"][:, 0:TD], st["B"][:, 0:TD]
            bA, bB = R.buf("A_sb"), R.buf("B_sb")
            R.op("dv", lambda e, o=Asl, i=ps_A: e.tensor_copy(o, i), reads=[pbb[0]], writes=[bA])
            R.op("dv", lambda e, o=Bsl, i=ps_B: e.tensor_copy(o, i), reads=[pbb[1]], writes=[bB])
            for k in range(CH):
                R.op("dv", (lambda e, o=xn2[:, k*TD:(k+1)*TD], a=xmb[:, k*TD:(k+1)*TD], b=Bsl:
                            e.tensor_tensor(o, a, b, ALU.add)),
                     reads=[bB, bxmb], writes=[bxn2])
                R.op("dv", (lambda e, o=xn2[:, k*TD:(k+1)*TD], b=Asl:
                            e.tensor_tensor(o, o, b, ALU.mult)),
                     reads=[bA], writes=[bxn2])

        def d_chain2(s):
            xn2 = xn2_b[s % 3]
            bxn2 = R.buf("xn2", s % 3)
            wb_sb, y_sb = wb_b[s % 2], y_b[s % 2]
            ps_r = pb[4][0:8, 0:TD]
            def g_r(e, ps=ps_r, xn2=xn2):
                for k in range(CH):
                    ins = e.matmul(ps, st["rw"][:, k*8:(k+1)*8], xn2[:, k*TD:(k+1)*TD],
                                   start=(k == 0), stop=(k == CH - 1))
                return ins
            R.op("pe", g_r, reads=[bxn2, bc["rw"]], writes=[pbb[4]])
            lg = st["zball"].bitcast(F32)[0:8, 128:128 + TD]
            blg = R.buf("lg")
            R.op("dv", lambda e, o=lg, i=ps_r: e.tensor_copy(o, i), reads=[pbb[4]], writes=[blg])
            ps_ltr = pb[4][0:128, TD:TD + 16]
            for j in range(2):
                R.op("pe", (lambda e, o=ps_ltr[:, j*8:(j+1)*8], i=lg[:, j*128:(j+1)*128]:
                            e.transpose(o, i, st["eyef"][0:8, 0:8])),
                     reads=[blg, bc["eyef"]], writes=[pbb[4]])
            lt = r4f[:, 1536:1552]
            blt = R.buf("lt")
            R.op("dv", lambda e, o=lt, i=ps_ltr: e.tensor_copy(o, i), reads=[pbb[4]], writes=[blt])
            s_sb = r4f[:, 1552:1568]
            w_sb = r4f[:, 1568:1584]
            sel = r4f[:, 1584:1600]
            bs, bw, bsel = R.buf("s_sb"), R.buf("w_sb"), R.buf("sel")
            for g in range(2):
                c8 = slice(g*8, (g+1)*8)
                mxc = r4f[:, 1600 + g: 1601 + g]
                m2c = r4f[:, 1602:1603]
                bmx, bm2 = R.buf("mx", g), R.buf("m2")
                R.op("dv", (lambda e, o=mxc, i=lt[:, c8]:
                            e.reduce_max(o, i, axis=AXX, negate=True)),
                     reads=[blt], writes=[bmx])
                R.op("ac", (lambda e, o=s_sb[:, c8], i=lt[:, c8], b=mxc:
                            e.activation(o, i, AF.Exp, bias=b)),
                     reads=[blt, bmx], writes=[bs])
                R.op("dv", (lambda e, o=m2c, i=s_sb[:, c8]: e.reduce_max(o, i, axis=AXX)),
                     reads=[bs], writes=[bm2])
                R.op("dv", (lambda e, o=sel[:, c8], i=s_sb[:, c8], m=m2c:
                            e.tensor_scalar(o, i, m, None, ALU.is_ge)),
                     reads=[bs, bm2], writes=[bsel])
                R.op("dv", (lambda e, o=sel[:, c8], a=sel[:, c8], b=s_sb[:, c8]:
                            e.scalar_tensor_tensor(o, a, -2.0, b, ALU.mult, ALU.add)),
                     reads=[bs], writes=[bsel])
                R.op("dv", (lambda e, o=m2c, i=sel[:, c8]: e.reduce_max(o, i, axis=AXX)),
                     reads=[bsel], writes=[bm2])
                R.op("dv", (lambda e, o=sel[:, c8], i=s_sb[:, c8], m=m2c:
                            e.tensor_scalar(o, i, m, None, ALU.is_ge)),
                     reads=[bs, bm2], writes=[bsel])
                R.op("dv", (lambda e, o=s_sb[:, c8], a=s_sb[:, c8], b=sel[:, c8]:
                            e.tensor_tensor(o, a, b, ALU.mult)),
                     reads=[bsel], writes=[bs])
                R.op("dv", (lambda e, o=m2c, i=s_sb[:, c8]: e.reduce_sum(o, i, axis=AXX)),
                     reads=[bs], writes=[bm2])
                R.op("dv", lambda e, o=m2c: e.reciprocal(o, o), reads=[bm2], writes=[bm2])
                R.op("dv", (lambda e, o=w_sb[:, c8], i=s_sb[:, c8], m=m2c:
                            e.tensor_scalar(o, i, m, 2.0, ALU.mult, ALU.mult)),
                     reads=[bs, bm2], writes=[bw])
            wbf = r4b[:, 3208:3224]
            bwbf = R.buf("wbf")
            R.op("dv", lambda e, o=wbf, i=w_sb: e.tensor_copy(o, i), reads=[bw], writes=[bwbf])
            ps_wtr = pb[5].bitcast(BF16)[0:8, 0:256]
            for j in range(2):
                R.op("pe", (lambda e, o=ps_wtr[0:8, j*128:(j+1)*128], i=wbf[:, j*8:(j+1)*8]:
                            e.transpose(o, i, st["eyeb"][:])),
                     reads=[bwbf, bc["eyeb"]], writes=[pbb[5]])
            wfm = st["zball"][0:8, 0:TD]
            bwfm = R.buf("wfm")
            R.op("dv", lambda e, o=wfm, i=ps_wtr: e.tensor_copy(o, i),
                 reads=[pbb[5]], writes=[bwfm])
            ps_wb = pb[6][:, 0:TD]
            R.op("pe", lambda e, o=ps_wb, r=wfm: e.matmul(o, st["d8"][:], r, start=True, stop=True),
                 reads=[bwfm, bc["d8"]], writes=[pbb[6]])
            bwb = R.buf("wb_sb", s % 2)
            R.op("dv", lambda e, o=wb_sb, i=ps_wb: e.tensor_copy(o, i),
                 reads=[pbb[6]], writes=[bwb])
            ps_ax = pb[7][:, 0:TD]
            def g_ax(e, ps=ps_ax, xn2=xn2):
                for k in range(CH):
                    ins = e.matmul(ps, st["la"][:, k*128:(k+1)*128], xn2[:, k*TD:(k+1)*TD],
                                   start=(k == 0), stop=(k == CH - 1))
                return ins
            R.op("pe", g_ax, reads=[bxn2, bc["la"]], writes=[pbb[7]])
            by = R.buf("y_sb", s % 2)
            R.op("dv", (lambda e, o=y_sb, i=ps_ax, w=wb_sb:
                        e.tensor_tensor(o, i, w, ALU.mult)),
                 reads=[pbb[7], bwb], writes=[by])

        def d_base(s, m_lo, m_hi):
            t0 = s * TD
            xn2, xmf = xn2_b[s % 3], xmf_b[s % 3]
            y_sb = y_b[s % 2]
            bxn2, bxmf = R.buf("xn2", s % 3), R.buf("xmf", s % 3)
            by = R.buf("y_sb", s % 2)
            for m in range(m_lo, m_hi):
                mdb = m % 2
                m4 = m % 4
                psb = pb[2 + mdb][:, 0:TD]
                pbuf = pbb[2 + mdb]
                def g_base(e, ps=psb, m=m, xn2=xn2, y_sb=y_sb):
                    for k in range(CH):
                        e.matmul(ps, fwT[:, k*2048 + m*128: k*2048 + (m+1)*128],
                                 xn2[:, k*TD:(k+1)*TD], start=(k == 0), stop=False)
                    return e.matmul(ps, st["lb"][:, m*128:(m+1)*128], y_sb,
                                    start=False, stop=True)
                R.op("pe", g_base,
                     reads=[bxn2, by, bc["lb"]] + [R.buf("fw", k) for k in range(CH)],
                     writes=[pbuf])
                om = r4f[:, m4*TD:(m4 + 1)*TD]
                bom = R.buf("om", m4)
                R.op("dv", (lambda e, o=om, i=psb, x=xmf[:, m*TD:(m+1)*TD]:
                            e.tensor_tensor(o, i, x, ALU.add)),
                     reads=[pbuf, bxmf], writes=[bom])
                R.dma("sp", out_d[m, :, t0:t0 + TD], om,
                      reads=[bom], writes=[R.buf("outd", m, s)])

        d_load_xmf(0)
        d_load_xmb(0)
        d_load_xmb(1)
        for s in range(NTD):
            ps_sq = d_chain1(s)
            if s >= 2:
                d_base(s - 2, 0, 6)
            d_chain1b(s, *ps_sq)
            d_load_xmb(s + 2)
            if s >= 2:
                d_base(s - 2, 6, 11)
            if s >= 1:
                d_chain2(s - 1)
            if s >= 2:
                d_base(s - 2, 11, 16)
            d_load_xmf(s + 1)
        d_chain2(NTD - 1)
        d_base(NTD - 2, 0, 16)
        d_base(NTD - 1, 0, 16)

        return finish()


def kernel(L0_attn_x, L0_attn_cos, L0_attn_sin, L0_attn_Wq, L0_attn_Wk,
           L0_attn_Wv, L0_attn_Wo, L0_ffn_weight, L0_ffn_lora_A,
           L0_ffn_lora_B, L0_ffn_router_w):
    _ = (L0_attn_cos, L0_attn_sin)
    x = np.asarray(L0_attn_x, np.float32)
    cm = lambda w: np.ascontiguousarray(np.asarray(w, np.float32).T).reshape(CH, P, 2048)
    wq, wk, wv, wo = cm(L0_attn_Wq), cm(L0_attn_Wk), cm(L0_attn_Wv), cm(L0_attn_Wo)
    fw = cm(L0_ffn_weight)
    la = np.ascontiguousarray(
        np.asarray(L0_ffn_lora_A, np.float32).reshape(128, 2048).T).reshape(CH, P, 128)
    lb = np.ascontiguousarray(
        np.asarray(L0_ffn_lora_B, np.float32).transpose(0, 2, 1).reshape(128, 2048))
    rw = np.ascontiguousarray(
        np.asarray(L0_ffn_router_w, np.float32).reshape(8, 16, 128).transpose(2, 1, 0).reshape(128, 128))
    d16 = np.zeros((16, 2048), np.float32)
    for h in range(16):
        d16[h, h*128:(h+1)*128] = 1.0
    d8 = np.zeros((8, 128), np.float32)
    for e8 in range(8):
        d8[e8, e8*16:(e8+1)*16] = 1.0
    eye = np.eye(128, dtype=np.float32)

    nc = build_program()
    in_maps = []
    for i in range(8):
        b, half = i // 2, i % 2
        xs = np.ascontiguousarray(x[b, half*2048:(half+1)*2048, :].T)
        in_maps.append(dict(x=xs, wq=wq, wk=wk, wv=wv, wo=wo, fw=fw, la=la,
                            lb=lb, rw=rw, d16=d16, d8=d8, eye=eye))
    res = run_bass_kernel_spmd(nc, in_maps, core_ids=list(range(8)))
    out = np.empty((4, 4096, 2048), np.float32)
    for i in range(8):
        b, half = i // 2, i % 2
        o = np.asarray(res.results[i]["out"]).reshape(2048, 2048)
        out[b, half*2048:(half+1)*2048, :] = o.T
    return out


# version 11
# speedup vs baseline: 1.1061x; 1.1061x over previous
import os
import numpy as np
import concourse.bass as bass
import concourse.mybir as mybir
from concourse.bass_utils import run_bass_kernel_spmd

F32 = mybir.dt.float32
BF16 = mybir.dt.bfloat16
AF = mybir.ActivationFunctionType
ALU = mybir.AluOpType
AXX = mybir.AxisListType.X

P = 128
CH = 16
TC = 2048
H = 16
TAU = 512
NT = TC // TAU
TD = 256
NTD = TC // TD
EPS = 1e-5

DBG = os.environ.get("KDBG", "")


class Buf:
    __slots__ = ("w", "r")
    def __init__(self):
        self.w = None
        self.r = []


class Rec:
    SEMS = ("pe", "dv", "ac", "po", "qsp", "qgp", "qac", "cc")
    ENGS = ("pe", "dv", "ac", "po", "sp")
    QUEUE = {"sp": ("sp", "qsp"), "gp": ("po", "qgp"), "ac": ("ac", "qac")}

    QENG = {"qsp": "sp", "qgp": "po", "qac": "ac"}

    def __init__(self):
        self.cnt = {s: 0 for s in self.SEMS}
        self.ops = {k: [] for k in self.ENGS}
        self._bufs = {}
        self.fence_cnt = {"qsp": 0, "qgp": 0, "qac": 0}

    def buf(self, *key):
        b = self._bufs.get(key)
        if b is None:
            b = self._bufs[key] = Buf()
        return b

    def _waits(self, reads, writes, extra):
        m = {}
        def add(ev):
            if ev is not None and ev[1] > m.get(ev[0], 0):
                m[ev[0]] = ev[1]
        for ev in extra:
            add(ev)
        for b in reads:
            add(b.w)
        for b in writes:
            add(b.w)
            for ev in b.r:
                add(ev)
        for s in ("qsp", "qgp", "qac"):
            if s in m:
                if m[s] > self.fence_cnt[s]:
                    self.ops[self.QENG[s]].append(([(s, self.cnt[s])], None, None, 0))
                    self.fence_cnt[s] = self.cnt[s]
                m[s] = self.fence_cnt[s]
        return sorted(m.items())

    def op(self, eng, fn, reads=(), writes=(), waits=(), sem=None, amt=1):
        sem = sem or eng
        w = self._waits(reads, writes, waits)
        self.cnt[sem] += amt
        ev = (sem, self.cnt[sem])
        self.ops[eng].append((w, fn, sem, amt))
        for b in writes:
            b.w = ev
            b.r = []
        for b in reads:
            b.r.append(ev)
        return ev

    def dma(self, q, out, in_, reads=(), writes=(), waits=()):
        eng, sem = self.QUEUE[q]
        def fn(e, out=out, in_=in_):
            return e.dma_start(out=out, in_=in_)
        return self.op(eng, fn, reads, writes, waits, sem=sem, amt=16)

    def wait_only(self, eng, waits):
        self.ops[eng].append((self._waits((), (), waits), None, None, 0))

    def mark(self, sem):
        return (sem, self.cnt[sem])

    def barrier(self):
        marks = [self.mark(s) for s in self.SEMS if self.cnt[s] > 0]
        for eng in self.ENGS:
            self.wait_only(eng, marks)


def emit(nc, rec, sems):
    with nc.Block() as block:
        def mk(key):
            def body(e):
                last = {}
                for (waits, fn, sem, amt) in rec.ops[key]:
                    for (s, v) in waits:
                        if last.get(s, 0) < v:
                            e.wait_ge(sems[s], v)
                            last[s] = v
                    if fn is None:
                        continue
                    ins = fn(e)
                    if amt:
                        ins.then_inc(sems[sem], amt)
            return body
        block.tensor(mk("pe"))
        block.vector(mk("dv"))
        block.scalar(mk("ac"))
        block.gpsimd(mk("po"))
        block.sync(mk("sp"))


def build_program():
    nc = bass.Bass("TRN2", target_bir_lowering=False, debug=False)

    x_d = nc.dram_tensor("x", [TC, TC], F32, kind="ExternalInput").ap()
    wq_d = nc.dram_tensor("wq", [CH, P, CH, 128], F32, kind="ExternalInput").ap()
    wk_d = nc.dram_tensor("wk", [CH, P, 2048], F32, kind="ExternalInput").ap()
    wv_d = nc.dram_tensor("wv", [CH, P, 2048], F32, kind="ExternalInput").ap()
    wo_d = nc.dram_tensor("wo", [CH, P, CH, 128], F32, kind="ExternalInput").ap()
    fw_d = nc.dram_tensor("fw", [CH, P, 2048], F32, kind="ExternalInput").ap()
    la_d = nc.dram_tensor("la", [CH, P, 128], F32, kind="ExternalInput").ap()
    lb_d = nc.dram_tensor("lb", [P, 2048], F32, kind="ExternalInput").ap()
    rw_d = nc.dram_tensor("rw", [P, 128], F32, kind="ExternalInput").ap()
    d16_d = nc.dram_tensor("d16", [16, 2048], F32, kind="ExternalInput").ap()
    d8_d = nc.dram_tensor("d8", [8, 128], F32, kind="ExternalInput").ap()
    eye_d = nc.dram_tensor("eye", [P, P], F32, kind="ExternalInput").ap()
    out_d = nc.dram_tensor("out", [CH, P, 2048], F32, kind="ExternalOutput").ap()
    xmid_d = nc.dram_tensor("xmid", [NTD, P, CH, TD], F32).ap()
    kvin_d = nc.dram_tensor("kvin", [P + 1, 2048], F32).ap()
    kvout_d = nc.dram_tensor("kvout", [P + 1, 2048], F32).ap()

    import contextlib
    es = contextlib.ExitStack()
    with es:
        r1 = nc.alloc_sbuf_tensor("r1", [P, 32768], BF16)
        r2 = nc.alloc_sbuf_tensor("r2", [P, 32768], BF16)
        r3 = nc.alloc_sbuf_tensor("r3", [P, 16384], BF16)
        r4 = nc.alloc_sbuf_tensor("r4", [P, 4096], BF16)
        r1b, r1f = r1.ap(), r1.ap().bitcast(F32)
        r2b = r2.ap()
        r3b = r3.ap()
        r4b, r4f = r4.ap(), r4.ap().bitcast(F32)

        phiq = r1b
        xn = r2b
        O = r1b
        fwT = r2b

        st = {}
        st["d16"] = nc.alloc_sbuf_tensor("d16b", [16, 2048], BF16).ap()
        st["d8"] = nc.alloc_sbuf_tensor("d8b", [8, 128], BF16).ap()
        st["eyeb"] = nc.alloc_sbuf_tensor("eyeb", [P, P], BF16).ap()
        st["eyef"] = nc.alloc_sbuf_tensor("eyef", [P, P], F32).ap()
        st["ones_c"] = nc.alloc_sbuf_tensor("ones_c", [P, 1], BF16).ap()
        st["ones_r"] = nc.alloc_sbuf_tensor("ones_r", [1, P], BF16).ap()
        st["rw"] = nc.alloc_sbuf_tensor("rwb", [P, 128], BF16).ap()
        st["la"] = nc.alloc_sbuf_tensor("lab", [P, 2048], BF16).ap()
        st["lb"] = nc.alloc_sbuf_tensor("lbb", [P, 2048], BF16).ap()
        st["wd"] = nc.alloc_sbuf_tensor("wdb", [P, 256], BF16).ap()
        st["kvb"] = nc.alloc_sbuf_tensor("kvb", [P, 2048], BF16).ap()
        st["kvp"] = nc.alloc_sbuf_tensor("kvp", [P, 2048], F32).ap()
        st["A"] = nc.alloc_sbuf_tensor("A_sb", [P, 512], BF16).ap()
        st["B"] = nc.alloc_sbuf_tensor("B_sb", [P, 512], BF16).ap()
        st["wby"] = nc.alloc_sbuf_tensor("wby", [P, 512], BF16).ap()
        st["zbs"] = nc.alloc_sbuf_tensor("zbs", [P, 1024], BF16).ap()
        st["ksr"] = nc.alloc_sbuf_tensor("ksr", [1, 2048], BF16).ap()
        st["zball"] = nc.alloc_sbuf_tensor("zball", [16, 2048], BF16).ap()
        st["eps"] = nc.alloc_sbuf_tensor("eps_c", [P, 1], F32).ap()
        nc.const_aps.register(F32, EPS, st["eps"][:])

        pb = [nc.alloc_psum_tensor(f"pb{i}", [P, 512], F32).ap() for i in range(8)]

        sems = {}
        for s in Rec.SEMS:
            sems[s] = es.enter_context(nc.semaphore(s))
        R = Rec()
        pbb = [R.buf("pb", i) for i in range(8)]

        def finish():
            R.wait_only("sp", [R.mark("qsp"), R.mark("qgp")])
            emit(nc, R, sems)
            return nc

        def dbg_dump_bf(src):
            R.barrier()
            for k in range(CH):
                R.dma("gp", out_d[k], src[:, k*2048:(k+1)*2048])
            return finish()

        def dbg_dump_kvp():
            R.barrier()
            R.dma("sp", out_d[0], st["kvp"][:, 0:2048])
            R.dma("sp", out_d[1, :, 0:16], st["kvp"][:, 2048:2064])
            return finish()

        bc = {}
        for nm, dst, src in [("d16", st["d16"][:], d16_d[:]),
                             ("d8", st["d8"][:], d8_d[:]),
                             ("eyeb", st["eyeb"][:], eye_d[:]),
                             ("rw", st["rw"][:], rw_d[:]),
                             ("lb", st["lb"][:], lb_d[:])]:
            bc[nm] = R.buf("c", nm)
            R.dma("gp", dst, src, writes=[bc[nm]])
        bc["eyef"] = R.buf("c", "eyef")
        R.dma("sp", st["eyef"][:], eye_d[:], writes=[bc["eyef"]])
        bc["la"] = R.buf("c", "la")
        for k in range(CH):
            R.dma("gp", st["la"][:, k*128:(k+1)*128], la_d[k], writes=[bc["la"]])
        bc["ones_c"] = R.buf("c", "ones_c")
        bc["ones_r"] = R.buf("c", "ones_r")
        R.op("po", lambda e: e.memset(st["ones_c"][:], 1.0), writes=[bc["ones_c"]])
        R.op("po", lambda e: e.memset(st["ones_r"][:], 1.0), writes=[bc["ones_r"]])
        bc["eps"] = R.buf("c", "eps")
        R.op("po", lambda e: e.memset(st["eps"][:], EPS), writes=[bc["eps"]])

        xnb = lambda k, it: R.buf("xn", k, it)
        for k in range(CH):
            R.dma("gp", xn[:, k*2048:(k+1)*2048], x_d[k*128:(k+1)*128, :],
                  writes=[xnb(k, 0), xnb(k, 1), xnb(k, 2), xnb(k, 3)])

        sq_a = r1b[:, 16512:24704]
        for it in range(NT):
            t0 = it * TAU
            xsl = lambda k, t0=t0: xn[:, k*2048 + t0: k*2048 + t0 + TAU]
            sqsl = lambda k: sq_a[:, k*512: k*512 + TAU]
            sqb = R.buf("sq_a")
            for k in range(CH):
                R.op("ac", (lambda e, i=xsl(k), o=sqsl(k): e.activation(o, i, AF.Square)),
                     reads=[xnb(k, it)], writes=[sqb])
            ps_s, ps_q = pb[0][0:1, 0:TAU], pb[1][0:1, 0:TAU]
            def g_s(e, ps=ps_s, xsl=xsl):
                for k in range(CH):
                    ins = e.matmul(ps, st["ones_c"][:], xsl(k),
                                   start=(k == 0), stop=(k == CH - 1))
                return ins
            def g_q(e, ps=ps_q, sqsl=sqsl):
                for k in range(CH):
                    ins = e.matmul(ps, st["ones_c"][:], sqsl(k),
                                   start=(k == 0), stop=(k == CH - 1))
                return ins
            R.op("pe", g_s, reads=[xnb(k, it) for k in range(CH)] + [bc["ones_c"]],
                 writes=[pbb[0]])
            R.op("pe", g_q, reads=[sqb, bc["ones_c"]], writes=[pbb[1]])
            rm = r4f[0:1, 0:TAU]
            rq = r4f[0:1, 512:512 + TAU]
            rt = r4f[0:1, 1024:1024 + TAU]
            abf = r4b[0:1, 3072:3072 + TAU]
            bbf = r4b[0:1, 3584:3584 + TAU]
            bm, bq, bt = R.buf("rm"), R.buf("rq"), R.buf("rt")
            babf, bbbf = R.buf("abf"), R.buf("bbf")
            R.op("dv", lambda e, o=rm, i=ps_s: e.tensor_scalar(o, i, 1.0/2048, None, ALU.mult),
                 reads=[pbb[0]], writes=[bm])
            R.op("dv", lambda e, o=rq, i=ps_q: e.tensor_scalar(o, i, 1.0/2048, None, ALU.mult),
                 reads=[pbb[1]], writes=[bq])
            R.op("dv", lambda e, o=rt, a=rm: e.tensor_tensor(o, a, a, ALU.mult),
                 reads=[bm], writes=[bt])
            R.op("dv", lambda e, o=rq, a=rq, b=rt: e.tensor_tensor(o, a, b, ALU.subtract),
                 reads=[bq, bt], writes=[bq])
            R.op("ac", lambda e, o=rt, i=rq: e.activation(o, i, AF.Ln, bias=EPS),
                 reads=[bq, bc["eps"]], writes=[bt])
            R.op("ac", lambda e, o=rt, i=rt: e.activation(o, i, AF.Exp, scale=-0.5),
                 reads=[bt], writes=[bt])
            R.op("dv", lambda e, o=rm, a=rm, b=rt:
                 e.scalar_tensor_tensor(o, a, -1.0, b, ALU.mult, ALU.mult),
                 reads=[bm, bt], writes=[bm])
            R.op("dv", lambda e, o=abf, i=rt: e.tensor_copy(o, i), reads=[bt], writes=[babf])
            R.op("dv", lambda e, o=bbf, i=rm: e.tensor_copy(o, i), reads=[bm], writes=[bbbf])
            ps_A, ps_B = pb[2][:, 0:TAU], pb[3][:, 0:TAU]
            R.op("pe", lambda e, o=ps_A, r=abf: e.matmul(o, st["ones_r"][:], r, start=True, stop=True),
                 reads=[babf, bc["ones_r"]], writes=[pbb[2]])
            R.op("pe", lambda e, o=ps_B, r=bbf: e.matmul(o, st["ones_r"][:], r, start=True, stop=True),
                 reads=[bbbf, bc["ones_r"]], writes=[pbb[3]])
            Asl, Bsl = st["A"][:, 0:TAU], st["B"][:, 0:TAU]
            bA, bB = R.buf("A_sb"), R.buf("B_sb")
            R.op("dv", lambda e, o=Asl, i=ps_A: e.tensor_copy(o, i), reads=[pbb[2]], writes=[bA])
            R.op("dv", lambda e, o=Bsl, i=ps_B: e.tensor_copy(o, i), reads=[pbb[3]], writes=[bB])
            for k in range(CH):
                R.op("dv", lambda e, a=xsl(k), b=Bsl: e.tensor_tensor(a, a, b, ALU.add),
                     reads=[bB], writes=[xnb(k, it)])
                R.op("dv", lambda e, a=xsl(k), b=Asl: e.tensor_tensor(a, a, b, ALU.mult),
                     reads=[bA], writes=[xnb(k, it)])

        if DBG == "A":
            return dbg_dump_bf(xn)

        wk_bufs = (r3b[:, 0:8192], r1b[:, 24576:32768])
        wv_bufs = (r3b[:, 8192:16384], r1b[:, 16384:24576])
        ev_a_ac, ev_a_pe = R.mark("ac"), R.mark("pe")
        fko = r1b[:, 0:8192]
        vo = r1b[:, 8192:16384]
        bfko, bvo = R.buf("fko"), R.buf("vo")
        KOT = int(os.environ.get("KOT", "4"))
        for ot in range(KOT):
            wk_sb = wk_bufs[ot % 2]
            wv_sb = wv_bufs[ot % 2]
            for tc in range(CH):
                db = tc % 2
                psk = pb[4 + db][:, 0:512]
                psv = pb[6 + db][:, 0:512]
                bk, bv = pbb[4 + db], pbb[6 + db]
                if tc == 8 and ot + 1 < KOT:
                    nk = wk_bufs[(ot + 1) % 2]
                    nv = wv_bufs[(ot + 1) % 2]
                    for k in range(CH):
                        R.dma("gp", nv[:, k*512:(k+1)*512],
                              wv_d[k, :, (ot+1)*512:(ot+2)*512],
                              writes=[R.buf("wv", (ot + 1) % 2, k)],
                              waits=[ev_a_ac, ev_a_pe] if (ot == 0 and k == 0) else ())
                        R.dma("gp", nk[:, k*512:(k+1)*512],
                              wk_d[k, :, (ot+1)*512:(ot+2)*512],
                              writes=[R.buf("wk", (ot + 1) % 2, k)])
                for j in range(4):
                    if tc == 0 and ot == 0:
                        for k in range(j*4, (j+1)*4):
                            R.dma("gp", wk_sb[:, k*512:(k+1)*512],
                                  wk_d[k, :, ot*512:(ot+1)*512], writes=[R.buf("wk", 0, k)])
                            R.dma("gp", wv_sb[:, k*512:(k+1)*512],
                                  wv_d[k, :, ot*512:(ot+1)*512], writes=[R.buf("wv", 0, k)])
                    def g_kv(e, pk=psk, pv=psv, wk_sb=wk_sb, wv_sb=wv_sb, j=j, tc=tc):
                        for k in range(j*4, j*4 + 4):
                            lhs = xn[:, k*2048 + tc*128: k*2048 + (tc+1)*128]
                            e.matmul(pk, lhs, wk_sb[:, k*512:(k+1)*512],
                                     start=(k == 0), stop=(k == CH - 1))
                            ins = e.matmul(pv, lhs, wv_sb[:, k*512:(k+1)*512],
                                           start=(k == 0), stop=(k == CH - 1))
                        return ins
                    R.op("pe", g_kv,
                         reads=[R.buf("wk", ot % 2, k) for k in range(j*4, j*4 + 4)]
                               + [R.buf("wv", ot % 2, k) for k in range(j*4, j*4 + 4)]
                               + [xnb(k, tc // 4) for k in range(j*4, j*4 + 4)],
                         writes=[bk, bv])
                t1 = st["zbs"][:, db*512:(db + 1)*512]
                bt1 = R.buf("bt1", db)
                R.op("dv", lambda e, o=t1, i=psk: e.tensor_scalar(o, i, 0.0, None, ALU.min),
                     reads=[bk], writes=[bt1])
                R.op("ac", lambda e, o=t1: e.activation(o, o, AF.Exp),
                     reads=[bt1], writes=[bt1])
                R.op("dv", (lambda e, o=fko[:, tc*512:(tc+1)*512], i=psk, x=t1:
                            e.scalar_tensor_tensor(o, i, 0.0, x, ALU.max, ALU.add)),
                     reads=[bk, bt1], writes=[bfko])
                R.op("dv", (lambda e, o=vo[:, tc*512:(tc+1)*512], i=psv: e.tensor_copy(o, i)),
                     reads=[bv], writes=[bvo])
            for i4 in range(4):
                pskv = pb[0 + i4][:, 0:128]
                def g_kvh(e, ps=pskv, i4=i4):
                    for tc in range(CH):
                        ins = e.matmul(ps, fko[:, tc*512 + i4*128: tc*512 + (i4+1)*128],
                                       vo[:, tc*512 + i4*128: tc*512 + (i4+1)*128],
                                       start=(tc == 0), stop=(tc == CH - 1))
                    return ins
                R.op("pe", g_kvh, reads=[bfko, bvo], writes=[pbb[0 + i4]])
            ps_ks = pb[4][0:1, 0:512]
            def g_ks(e, ps=ps_ks):
                for tc in range(CH):
                    ins = e.matmul(ps, st["ones_c"][:], fko[:, tc*512:(tc+1)*512],
                                   start=(tc == 0), stop=(tc == CH - 1))
                return ins
            R.op("pe", g_ks, reads=[bfko, bc["ones_c"]], writes=[pbb[4]])
            for i4 in range(4):
                hh = ot*4 + i4
                R.op("dv", (lambda e, o=st["kvp"][:, hh*128:(hh+1)*128], i=pb[0+i4][:, 0:128]:
                            e.tensor_copy(o, i)),
                     reads=[pbb[0 + i4]], writes=[R.buf("kvp")])
            R.op("dv", (lambda e, o=st["ksr"][0:1, ot*512:(ot+1)*512], i=ps_ks:
                        e.tensor_copy(o, i)),
                 reads=[pbb[4]], writes=[R.buf("ksr")])

        if DBG == "B2F":
            R.barrier()
            for k in range(4):
                R.dma("gp", out_d[k], fko[:, k*2048:(k+1)*2048])
            for k in range(4):
                R.dma("gp", out_d[4 + k], vo[:, k*2048:(k+1)*2048])
            return finish()
        if DBG == "B2":
            return dbg_dump_kvp()
        R.barrier()

        R.dma("sp", kvin_d[0:P], st["kvp"][:], reads=[R.buf("kvp")], writes=[R.buf("kvin")])
        R.dma("gp", kvin_d[P:P+1], st["ksr"][:], reads=[R.buf("ksr")], writes=[R.buf("kvin")])
        R.op("po", lambda e: e.collective_compute(
                "AllReduce", ALU.add,
                replica_groups=[[0, 1], [2, 3], [4, 5], [6, 7]],
                ins=[kvin_d[:]], outs=[kvout_d[:]]),
             reads=[R.buf("kvin")], writes=[R.buf("kvout")], sem="cc", amt=1)

        for h in range(H):
            db = h % 2
            base = db * 4
            wq_ap = r3b[:, db*2048:(db + 1)*2048]
            wb = R.buf("wq", db)
            R.dma("gp", wq_ap, wq_d[h], writes=[wb])
            def g_qp(e, w=wq_ap, base=base):
                for k in range(CH):
                    for itau in range(NT):
                        ins = e.matmul(pb[base + itau][:, 0:TAU], w[:, k*128:(k+1)*128],
                                       xn[:, k*2048 + itau*TAU: k*2048 + itau*TAU + TAU],
                                       start=(k == 0), stop=(k == CH - 1))
                return ins
            R.op("pe", g_qp,
                 reads=[wb] + [xnb(k, i2) for k in range(CH) for i2 in range(NT)],
                 writes=[pbb[base + i] for i in range(NT)])
            for itau in range(NT):
                psq = pb[base + itau][:, 0:TAU]
                pbuf = pbb[base + itau]
                t1 = r4b[:, (itau % 2)*512:(itau % 2)*512 + TAU]
                e1 = r4b[:, 1024 + (itau % 2)*512:1024 + (itau % 2)*512 + TAU]
                bt1, be1 = R.buf("t1", itau % 2), R.buf("e1", itau % 2)
                R.op("dv", lambda e, o=t1, i=psq: e.tensor_scalar(o, i, 0.0, None, ALU.min),
                     reads=[pbuf], writes=[bt1])
                R.op("ac", lambda e, o=e1, i=t1: e.activation(o, i, AF.Exp),
                     reads=[bt1], writes=[be1])
                phis = phiq[:, h*2048 + itau*TAU: h*2048 + itau*TAU + TAU]
                R.op("dv", lambda e, o=phis, i=psq, x=e1:
                     e.scalar_tensor_tensor(o, i, 0.0, x, ALU.max, ALU.add),
                     reads=[pbuf, be1], writes=[R.buf("phiq", h, itau)])

        if DBG == "B1":
            return dbg_dump_bf(phiq)
        ev_b1_pe = R.mark("pe")

        R.dma("sp", st["kvp"][:], kvout_d[0:P], reads=[R.buf("kvout")], writes=[R.buf("kvp")])
        R.dma("gp", st["ksr"][:], kvout_d[P:P+1], reads=[R.buf("kvout")], writes=[R.buf("ksr")])
        R.op("dv", (lambda e, o=st["kvb"][:], i=st["kvp"][:]: e.tensor_copy(o, i)),
             reads=[R.buf("kvp")], writes=[R.buf("kvb")])
        R.op("po", lambda e: e.memset(st["wd"][:], 0.0), writes=[R.buf("wd")])
        for h in range(H):
            pst = pb[5].bitcast(BF16)[:, 0:1]
            R.op("pe", (lambda e, o=pst, i=st["ksr"][0:1, h*128:(h+1)*128]:
                        e.transpose(o, i, st["eyeb"][0:1, 0:1])),
                 reads=[R.buf("ksr"), bc["eyeb"]], writes=[pbb[5]])
            R.op("dv", (lambda e, o=st["wd"][:, h*16+h: h*16+h+1], i=pst:
                        e.tensor_copy(o, i)),
                 reads=[pbb[5]], writes=[R.buf("wd")])

        if DBG == "AG":
            return dbg_dump_kvp()
        R.barrier()

        for m in range(4):
            wo_ap = r3b[:, (m % 4)*2048:((m % 4) + 1)*2048]
            wb = R.buf("wo", m % 4)
            R.dma("gp", wo_ap, wo_d[m], writes=[wb],
                  waits=[ev_b1_pe] if m == 0 else ())
        for k in range(CH):
            R.dma("gp", fwT[:, k*2048:(k+1)*2048], fw_d[k], writes=[R.buf("fw", k)])

        zball = st["zball"]
        bzall = R.buf("zball")
        for it in range(NT):
            t0 = it * TAU
            ps_den = pb[0][0:16, 0:TAU]
            def g_den(e, ps=ps_den, t0=t0):
                for k in range(CH):
                    ins = e.matmul(ps, st["wd"][:, k*16:(k+1)*16],
                                   phiq[:, k*2048 + t0: k*2048 + t0 + TAU],
                                   start=(k == 0), stop=(k == CH - 1))
                return ins
            R.op("pe", g_den,
                 reads=[R.buf("wd")] + [R.buf("phiq", h, it) for h in range(H)],
                 writes=[pbb[0]])
            zr = r4f[0:16, 512:512 + TAU]
            bz = R.buf("zr")
            R.op("dv", lambda e, o=zr, i=ps_den: e.tensor_scalar(o, i, 1e-6, None, ALU.add),
                 reads=[pbb[0]], writes=[bz])
            R.op("dv", lambda e, o=zr: e.reciprocal(o, o), reads=[bz], writes=[bz])
            R.op("dv", (lambda e, o=zball[:, it*TAU:(it+1)*TAU], i=zr: e.tensor_copy(o, i)),
                 reads=[bz], writes=[bzall])
        for it in range(NT):
            t0 = it * TAU
            for h in range(H):
                db = h % 2
                ps_zb = pb[1 + db][:, 0:TAU]
                ps_at = pb[3 + db][:, 0:TAU]
                bzb, bat = pbb[1 + db], pbb[3 + db]
                R.op("pe", (lambda e, o=ps_zb, l=st["d16"][:, h*128:(h+1)*128],
                            r=zball[:, it*TAU:(it+1)*TAU]:
                            e.matmul(o, l, r, start=True, stop=True)),
                     reads=[bzall, bc["d16"]], writes=[bzb])
                R.op("pe", (lambda e, o=ps_at, l=st["kvb"][:, h*128:(h+1)*128],
                            r=phiq[:, h*2048 + t0: h*2048 + t0 + TAU]:
                            e.matmul(o, l, r, start=True, stop=True)),
                     reads=[R.buf("kvb"), R.buf("phiq", h, it)], writes=[bat])
                zbs = st["zbs"][:, db*512: db*512 + TAU]
                bzbs = R.buf("zbs", db)
                R.op("dv", lambda e, o=zbs, i=ps_zb: e.tensor_copy(o, i),
                     reads=[bzb], writes=[bzbs])
                osl = O[:, h*2048 + t0: h*2048 + t0 + TAU]
                R.op("dv", lambda e, o=osl, i=ps_at, z=zbs: e.tensor_tensor(o, i, z, ALU.mult),
                     reads=[bat, bzbs], writes=[R.buf("phiq", h, it)])

        if DBG == "C1":
            return dbg_dump_bf(O)

        for m in range(CH):
            db = m % 4
            wo_ap = r3b[:, db*2048:(db + 1)*2048]
            wb = R.buf("wo", db)
            if m >= 4:
                R.dma("gp", wo_ap, wo_d[m], writes=[wb])

            for it in range(NT):
                t0 = it * TAU
                idx = m * NT + it
                def xt_load(ix):
                    if ix >= CH * NT:
                        return
                    mm, ii = ix // NT, ix % NT
                    xt_ = r3b.bitcast(F32)[:, 4096 + (ix % 4)*512: 4096 + (ix % 4)*512 + TAU]
                    R.dma("ac", xt_, x_d[mm*128:(mm+1)*128, ii*TAU:(ii+1)*TAU],
                          writes=[R.buf("xt", ix % 4)])
                if idx == 0:
                    for ix in range(3):
                        xt_load(ix)
                xt_load(idx + 3)
                ps = pb[5 + (it % 2)][:, 0:TAU]
                pbuf = pbb[5 + (it % 2)]
                def g_wo(e, ps=ps, w=wo_ap, t0=t0):
                    for k in range(CH):
                        ins = e.matmul(ps, w[:, k*128:(k+1)*128],
                                       O[:, k*2048 + t0: k*2048 + t0 + TAU],
                                       start=(k == 0), stop=(k == CH - 1))
                    return ins
                R.op("pe", g_wo, reads=[wb] + [R.buf("phiq", k, it) for k in range(CH)],
                     writes=[pbuf])
                xt = r3b.bitcast(F32)[:, 4096 + (idx % 4)*512: 4096 + (idx % 4)*512 + TAU]
                xm = r3b.bitcast(F32)[:, 6144 + (idx % 4)*512: 6144 + (idx % 4)*512 + TAU]
                bxt, bxm = R.buf("xt", idx % 4), R.buf("xm", idx % 4)
                R.op("dv", lambda e, o=xm, i=ps, x=xt: e.tensor_tensor(o, i, x, ALU.add),
                     reads=[pbuf, bxt], writes=[bxm])
                R.dma("sp",
                      xmid_d[2*it:2*it + 2, :, m, :].rearrange("s p t -> p s t"),
                      xm, reads=[bxm], writes=[R.buf("xmid_d", m)])

        if DBG == "C2":
            R.barrier()
            for m in range(CH):
                R.dma("sp", out_d[m],
                      xmid_d[:, :, m, :].rearrange("s p t -> p s t"),
                      reads=[R.buf("xmid_d", m)])
            return finish()
        R.barrier()

        sqd = r3b[:, 12288:16384]
        xn2_b = [r3b[:, 0:4096], r3b[:, 4096:8192], r3b[:, 8192:12288]]
        xmf_b = [r1f[:, 0:4096], r1f[:, 4096:8192], r1f[:, 8192:12288]]
        xmb_b = [r1b[:, 24576:28672], r1b[:, 28672:32768]]
        wb_b = [r4b[:, 2048:2304], r4b[:, 2304:2560]]
        y_b = [r4b[:, 2560:2816], r4b[:, 2816:3072]]
        rows_f = st["ksr"].bitcast(F32)

        def d_load_xmf(s):
            if s >= NTD:
                return
            xmf, bxmf = xmf_b[s % 3], R.buf("xmf", s % 3)
            R.dma("ac", xmf, xmid_d[s],
                  reads=[R.buf("xmid_d", m) for m in range(CH)], writes=[bxmf])

        def d_load_xmb(s):
            if s >= NTD:
                return
            xmb, bxmb = xmb_b[s % 2], R.buf("xmb", s % 2)
            R.dma("gp", xmb, xmid_d[s],
                  reads=[R.buf("xmid_d", m) for m in range(CH)], writes=[bxmb])

        def d_chain1(s):
            xmf, xmb, xn2 = xmf_b[s % 3], xmb_b[s % 2], xn2_b[s % 3]
            bxmf, bxmb, bxn2 = R.buf("xmf", s % 3), R.buf("xmb", s % 2), R.buf("xn2", s % 3)
            bsqd = R.buf("sqd")
            for k in range(CH):
                R.op("ac", (lambda e, i=xmb[:, k*TD:(k+1)*TD], o=sqd[:, k*TD:(k+1)*TD]:
                            e.activation(o, i, AF.Square)),
                     reads=[bxmb], writes=[bsqd])
            ps_s, ps_q = pb[0][0:1, 0:TD], pb[1][0:1, 0:TD]
            def g_s2(e, ps=ps_s, xmb=xmb):
                for k in range(CH):
                    ins = e.matmul(ps, st["ones_c"][:], xmb[:, k*TD:(k+1)*TD],
                                   start=(k == 0), stop=(k == CH - 1))
                return ins
            def g_q2(e, ps=ps_q, sqd=sqd):
                for k in range(CH):
                    ins = e.matmul(ps, st["ones_c"][:], sqd[:, k*TD:(k+1)*TD],
                                   start=(k == 0), stop=(k == CH - 1))
                return ins
            R.op("pe", g_s2, reads=[bxmb, bc["ones_c"]], writes=[pbb[0]])
            R.op("pe", g_q2, reads=[bsqd, bc["ones_c"]], writes=[pbb[1]])
            return (ps_s, ps_q)

        def d_chain1b(s, ps_s, ps_q):
            xmf, xmb, xn2 = xmf_b[s % 3], xmb_b[s % 2], xn2_b[s % 3]
            bxmb, bxn2 = R.buf("xmb", s % 2), R.buf("xn2", s % 3)
            rm = rows_f[0:1, 0:TD]
            rq = rows_f[0:1, 256:256 + TD]
            rt = rows_f[0:1, 512:512 + TD]
            abf = r4b[0:1, 3224:3224 + TD]
            bbf = r4b[0:1, 3480:3480 + TD]
            bm, bq, bt = R.buf("rm2"), R.buf("rq2"), R.buf("rt2")
            babf, bbbf = R.buf("abf2"), R.buf("bbf2")
            R.op("dv", lambda e, o=rm, i=ps_s: e.tensor_scalar(o, i, 1.0/2048, None, ALU.mult),
                 reads=[pbb[0]], writes=[bm])
            R.op("dv", lambda e, o=rq, i=ps_q: e.tensor_scalar(o, i, 1.0/2048, None, ALU.mult),
                 reads=[pbb[1]], writes=[bq])
            R.op("dv", lambda e, o=rt, a=rm: e.tensor_tensor(o, a, a, ALU.mult),
                 reads=[bm], writes=[bt])
            R.op("dv", lambda e, o=rq, a=rq, b=rt: e.tensor_tensor(o, a, b, ALU.subtract),
                 reads=[bq, bt], writes=[bq])
            R.op("ac", lambda e, o=rt, i=rq: e.activation(o, i, AF.Ln, bias=EPS),
                 reads=[bq, bc["eps"]], writes=[bt])
            R.op("ac", lambda e, o=rt: e.activation(o, o, AF.Exp, scale=-0.5),
                 reads=[bt], writes=[bt])
            R.op("dv", lambda e, o=rm, a=rm, b=rt:
                 e.scalar_tensor_tensor(o, a, -1.0, b, ALU.mult, ALU.mult),
                 reads=[bm, bt], writes=[bm])
            R.op("dv", lambda e, o=abf, i=rt: e.tensor_copy(o, i), reads=[bt], writes=[babf])
            R.op("dv", lambda e, o=bbf, i=rm: e.tensor_copy(o, i), reads=[bm], writes=[bbbf])
            ps_A, ps_B = pb[0][:, 0:TD], pb[1][:, 0:TD]
            R.op("pe", lambda e, o=ps_A, r=abf: e.matmul(o, st["ones_r"][:], r, start=True, stop=True),
                 reads=[babf, bc["ones_r"]], writes=[pbb[0]])
            R.op("pe", lambda e, o=ps_B, r=bbf: e.matmul(o, st["ones_r"][:], r, start=True, stop=True),
                 reads=[bbbf, bc["ones_r"]], writes=[pbb[1]])
            Asl, Bsl = st["A"][:, 0:TD], st["B"][:, 0:TD]
            bA, bB = R.buf("A_sb"), R.buf("B_sb")
            R.op("dv", lambda e, o=Asl, i=ps_A: e.tensor_copy(o, i), reads=[pbb[0]], writes=[bA])
            R.op("dv", lambda e, o=Bsl, i=ps_B: e.tensor_copy(o, i), reads=[pbb[1]], writes=[bB])
            for k in range(CH):
                R.op("dv", (lambda e, o=xn2[:, k*TD:(k+1)*TD], a=xmb[:, k*TD:(k+1)*TD], b=Bsl:
                            e.tensor_tensor(o, a, b, ALU.add)),
                     reads=[bB, bxmb], writes=[bxn2])
                R.op("dv", (lambda e, o=xn2[:, k*TD:(k+1)*TD], b=Asl:
                            e.tensor_tensor(o, o, b, ALU.mult)),
                     reads=[bA], writes=[bxn2])

        def d_chain2(s):
            xn2 = xn2_b[s % 3]
            bxn2 = R.buf("xn2", s % 3)
            wb_sb, y_sb = wb_b[s % 2], y_b[s % 2]
            ps_r = pb[4][0:8, 0:TD]
            def g_r(e, ps=ps_r, xn2=xn2):
                for k in range(CH):
                    ins = e.matmul(ps, st["rw"][:, k*8:(k+1)*8], xn2[:, k*TD:(k+1)*TD],
                                   start=(k == 0), stop=(k == CH - 1))
                return ins
            R.op("pe", g_r, reads=[bxn2, bc["rw"]], writes=[pbb[4]])
            lg = st["zball"].bitcast(F32)[0:8, 128:128 + TD]
            blg = R.buf("lg")
            R.op("dv", lambda e, o=lg, i=ps_r: e.tensor_copy(o, i), reads=[pbb[4]], writes=[blg])
            ps_ltr = pb[4][0:128, TD:TD + 16]
            for j in range(2):
                R.op("pe", (lambda e, o=ps_ltr[:, j*8:(j+1)*8], i=lg[:, j*128:(j+1)*128]:
                            e.transpose(o, i, st["eyef"][0:8, 0:8])),
                     reads=[blg, bc["eyef"]], writes=[pbb[4]])
            lt = r4f[:, 1536:1552]
            blt = R.buf("lt")
            R.op("dv", lambda e, o=lt, i=ps_ltr: e.tensor_copy(o, i), reads=[pbb[4]], writes=[blt])
            s_sb = r4f[:, 1552:1568]
            w_sb = r4f[:, 1568:1584]
            sel = r4f[:, 1584:1600]
            bs, bw, bsel = R.buf("s_sb"), R.buf("w_sb"), R.buf("sel")
            for g in range(2):
                c8 = slice(g*8, (g+1)*8)
                mxc = r4f[:, 1600 + g: 1601 + g]
                m2c = r4f[:, 1602:1603]
                bmx, bm2 = R.buf("mx", g), R.buf("m2")
                R.op("dv", (lambda e, o=mxc, i=lt[:, c8]:
                            e.reduce_max(o, i, axis=AXX, negate=True)),
                     reads=[blt], writes=[bmx])
                R.op("ac", (lambda e, o=s_sb[:, c8], i=lt[:, c8], b=mxc:
                            e.activation(o, i, AF.Exp, bias=b)),
                     reads=[blt, bmx], writes=[bs])
                R.op("dv", (lambda e, o=m2c, i=s_sb[:, c8]: e.reduce_max(o, i, axis=AXX)),
                     reads=[bs], writes=[bm2])
                R.op("dv", (lambda e, o=sel[:, c8], i=s_sb[:, c8], m=m2c:
                            e.tensor_scalar(o, i, m, None, ALU.is_ge)),
                     reads=[bs, bm2], writes=[bsel])
                R.op("dv", (lambda e, o=sel[:, c8], a=sel[:, c8], b=s_sb[:, c8]:
                            e.scalar_tensor_tensor(o, a, -2.0, b, ALU.mult, ALU.add)),
                     reads=[bs], writes=[bsel])
                R.op("dv", (lambda e, o=m2c, i=sel[:, c8]: e.reduce_max(o, i, axis=AXX)),
                     reads=[bsel], writes=[bm2])
                R.op("dv", (lambda e, o=sel[:, c8], i=s_sb[:, c8], m=m2c:
                            e.tensor_scalar(o, i, m, None, ALU.is_ge)),
                     reads=[bs, bm2], writes=[bsel])
                R.op("dv", (lambda e, o=s_sb[:, c8], a=s_sb[:, c8], b=sel[:, c8]:
                            e.tensor_tensor(o, a, b, ALU.mult)),
                     reads=[bsel], writes=[bs])
                R.op("dv", (lambda e, o=m2c, i=s_sb[:, c8]: e.reduce_sum(o, i, axis=AXX)),
                     reads=[bs], writes=[bm2])
                R.op("dv", lambda e, o=m2c: e.reciprocal(o, o), reads=[bm2], writes=[bm2])
                R.op("dv", (lambda e, o=w_sb[:, c8], i=s_sb[:, c8], m=m2c:
                            e.tensor_scalar(o, i, m, 2.0, ALU.mult, ALU.mult)),
                     reads=[bs, bm2], writes=[bw])
            wbf = r4b[:, 3208:3224]
            bwbf = R.buf("wbf")
            R.op("dv", lambda e, o=wbf, i=w_sb: e.tensor_copy(o, i), reads=[bw], writes=[bwbf])
            ps_wtr = pb[5].bitcast(BF16)[0:8, 0:256]
            for j in range(2):
                R.op("pe", (lambda e, o=ps_wtr[0:8, j*128:(j+1)*128], i=wbf[:, j*8:(j+1)*8]:
                            e.transpose(o, i, st["eyeb"][:])),
                     reads=[bwbf, bc["eyeb"]], writes=[pbb[5]])
            wfm = st["zball"][0:8, 0:TD]
            bwfm = R.buf("wfm")
            R.op("dv", lambda e, o=wfm, i=ps_wtr: e.tensor_copy(o, i),
                 reads=[pbb[5]], writes=[bwfm])
            ps_wb = pb[6][:, 0:TD]
            R.op("pe", lambda e, o=ps_wb, r=wfm: e.matmul(o, st["d8"][:], r, start=True, stop=True),
                 reads=[bwfm, bc["d8"]], writes=[pbb[6]])
            bwb = R.buf("wb_sb", s % 2)
            R.op("dv", lambda e, o=wb_sb, i=ps_wb: e.tensor_copy(o, i),
                 reads=[pbb[6]], writes=[bwb])
            ps_ax = pb[7][:, 0:TD]
            def g_ax(e, ps=ps_ax, xn2=xn2):
                for k in range(CH):
                    ins = e.matmul(ps, st["la"][:, k*128:(k+1)*128], xn2[:, k*TD:(k+1)*TD],
                                   start=(k == 0), stop=(k == CH - 1))
                return ins
            R.op("pe", g_ax, reads=[bxn2, bc["la"]], writes=[pbb[7]])
            by = R.buf("y_sb", s % 2)
            R.op("dv", (lambda e, o=y_sb, i=ps_ax, w=wb_sb:
                        e.tensor_tensor(o, i, w, ALU.mult)),
                 reads=[pbb[7], bwb], writes=[by])

        def d_base(s, m_lo, m_hi):
            t0 = s * TD
            xn2, xmf = xn2_b[s % 3], xmf_b[s % 3]
            y_sb = y_b[s % 2]
            bxn2, bxmf = R.buf("xn2", s % 3), R.buf("xmf", s % 3)
            by = R.buf("y_sb", s % 2)
            for m in range(m_lo, m_hi):
                mdb = m % 2
                m4 = m % 4
                psb = pb[2 + mdb][:, 0:TD]
                pbuf = pbb[2 + mdb]
                def g_base(e, ps=psb, m=m, xn2=xn2, y_sb=y_sb):
                    for k in range(CH):
                        e.matmul(ps, fwT[:, k*2048 + m*128: k*2048 + (m+1)*128],
                                 xn2[:, k*TD:(k+1)*TD], start=(k == 0), stop=False)
                    return e.matmul(ps, st["lb"][:, m*128:(m+1)*128], y_sb,
                                    start=False, stop=True)
                R.op("pe", g_base,
                     reads=[bxn2, by, bc["lb"]] + [R.buf("fw", k) for k in range(CH)],
                     writes=[pbuf])
                om = r4f[:, m4*TD:(m4 + 1)*TD]
                bom = R.buf("om", m4)
                R.op("dv", (lambda e, o=om, i=psb, x=xmf[:, m*TD:(m+1)*TD]:
                            e.tensor_tensor(o, i, x, ALU.add)),
                     reads=[pbuf, bxmf], writes=[bom])
                R.dma("sp", out_d[m, :, t0:t0 + TD], om,
                      reads=[bom], writes=[R.buf("outd", m, s)])

        d_load_xmf(0)
        d_load_xmb(0)
        d_load_xmb(1)
        for s in range(NTD):
            ps_sq = d_chain1(s)
            if s >= 2:
                d_base(s - 2, 0, 6)
            d_chain1b(s, *ps_sq)
            d_load_xmb(s + 2)
            if s >= 2:
                d_base(s - 2, 6, 11)
            if s >= 1:
                d_chain2(s - 1)
            if s >= 2:
                d_base(s - 2, 11, 16)
            d_load_xmf(s + 1)
        d_chain2(NTD - 1)
        d_base(NTD - 2, 0, 16)
        d_base(NTD - 1, 0, 16)

        return finish()


def kernel(L0_attn_x, L0_attn_cos, L0_attn_sin, L0_attn_Wq, L0_attn_Wk,
           L0_attn_Wv, L0_attn_Wo, L0_ffn_weight, L0_ffn_lora_A,
           L0_ffn_lora_B, L0_ffn_router_w):
    _ = (L0_attn_cos, L0_attn_sin)
    x = np.asarray(L0_attn_x, np.float32)
    cm = lambda w: np.ascontiguousarray(np.asarray(w, np.float32).T).reshape(CH, P, 2048)
    colm = lambda w: np.ascontiguousarray(
        np.asarray(w, np.float32).T.reshape(CH, P, CH, 128).transpose(2, 1, 0, 3))
    wq, wo = colm(L0_attn_Wq), colm(L0_attn_Wo)
    wk, wv = cm(L0_attn_Wk), cm(L0_attn_Wv)
    fw = cm(L0_ffn_weight)
    la = np.ascontiguousarray(
        np.asarray(L0_ffn_lora_A, np.float32).reshape(128, 2048).T).reshape(CH, P, 128)
    lb = np.ascontiguousarray(
        np.asarray(L0_ffn_lora_B, np.float32).transpose(0, 2, 1).reshape(128, 2048))
    rw = np.ascontiguousarray(
        np.asarray(L0_ffn_router_w, np.float32).reshape(8, 16, 128).transpose(2, 1, 0).reshape(128, 128))
    d16 = np.zeros((16, 2048), np.float32)
    for h in range(16):
        d16[h, h*128:(h+1)*128] = 1.0
    d8 = np.zeros((8, 128), np.float32)
    for e8 in range(8):
        d8[e8, e8*16:(e8+1)*16] = 1.0
    eye = np.eye(128, dtype=np.float32)

    nc = build_program()
    in_maps = []
    for i in range(8):
        b, half = i // 2, i % 2
        xs = np.ascontiguousarray(x[b, half*2048:(half+1)*2048, :].T)
        in_maps.append(dict(x=xs, wq=wq, wk=wk, wv=wv, wo=wo, fw=fw, la=la,
                            lb=lb, rw=rw, d16=d16, d8=d8, eye=eye))
    res = run_bass_kernel_spmd(nc, in_maps, core_ids=list(range(8)))
    out = np.empty((4, 4096, 2048), np.float32)
    for i in range(8):
        b, half = i // 2, i % 2
        o = np.asarray(res.results[i]["out"]).reshape(2048, 2048)
        out[b, half*2048:(half+1)*2048, :] = o.T
    return out
